# version 01
# baseline (speedup 1.0000x reference)
import math
import numpy as np
import concourse.bass as bass
import concourse.mybir as mybir
from concourse.bass_utils import run_bass_kernel_spmd

F32 = mybir.dt.float32
BF16 = mybir.dt.bfloat16
I32 = mybir.dt.int32
AX = mybir.AxisListType.X
OP = mybir.AluOpType
AF = mybir.ActivationFunctionType

S = 2048
D = 2048
H = 16
DH = 128
DFF = 8192
NT = S // 128
NC = D // 128
NDFF = DFF // 128
SEG = 512
NSEG = S // SEG
ISQ = 1.0 / math.sqrt(128.0)
NEGBIG = -1.0e30

ENGS = ("t", "v", "s", "qg", "qs")
BLOCKNAME = {"t": "tensor", "v": "vector", "s": "scalar", "qg": "gpsimd", "qs": "sync"}


class Buf:
    __slots__ = ("last_w", "readers", "name")

    def __init__(self, name):
        self.name = name
        self.last_w = None
        self.readers = {}


class Tile:
    __slots__ = ("t", "b")

    def __init__(self, t, b):
        self.t = t
        self.b = b

    def __getitem__(self, idx):
        return self.t[idx]


class Prog:
    def __init__(self, nc):
        self.nc = nc
        self.items = {e: [] for e in ENGS}
        self.seq = {e: 0 for e in ENGS}
        self.inc_count = {e: 0 for e in ENGS}
        self.marks = {e: [] for e in ENGS}
        self.vclock = {e: {f: 0 for f in ENGS} for e in ENGS}

    def _cover(self, e, seq):
        marks = self.marks[e]
        lo, hi = 0, len(marks)
        while lo < hi:
            mid = (lo + hi) // 2
            if marks[mid][0] >= seq:
                hi = mid
            else:
                lo = mid + 1
        if lo < len(marks):
            return marks[lo][1]
        assert e not in ("qg", "qs"), f"uncovered DMA seq {e}:{seq}"
        self.inc_count[e] += 1
        self.items[e].append(("n",))
        marks.append((self.seq[e], self.inc_count[e]))
        return self.inc_count[e]

    def _wait_for(self, eng, dep_eng, dep_seq):
        if dep_eng == eng and eng not in ("qg", "qs"):
            return
        val = self._cover(dep_eng, dep_seq)
        if self.vclock[eng][dep_eng] < val:
            self.items[eng].append(("w", dep_eng, val))
            self.vclock[eng][dep_eng] = val

    def emit(self, eng, fn, reads=(), writes=(), inc=None):
        if eng in ("v", "s"):
            for x in reads:
                b = x.b if isinstance(x, Tile) else x
                if b.last_w == (eng, self.seq[eng]):
                    self.items[eng].append(("d",))
                    break
        for x in reads:
            b = x.b if isinstance(x, Tile) else x
            if b.last_w:
                self._wait_for(eng, *b.last_w)
        for x in writes:
            b = x.b if isinstance(x, Tile) else x
            if b.last_w:
                self._wait_for(eng, *b.last_w)
            for de, ds in b.readers.items():
                self._wait_for(eng, de, ds)
        self.seq[eng] += 1
        myseq = self.seq[eng]
        if inc is None:
            inc = eng != "t"
        if eng in ("qg", "qs"):
            inc = True
        if inc:
            self.inc_count[eng] += 16 if eng in ("qg", "qs") else 1
            self.marks[eng].append((myseq, self.inc_count[eng]))
        self.items[eng].append(("i", fn, inc))
        for x in reads:
            b = x.b if isinstance(x, Tile) else x
            b.readers[eng] = myseq
        for x in writes:
            b = x.b if isinstance(x, Tile) else x
            b.last_w = (eng, myseq)
            b.readers = {}
        return myseq

    def finalize_into_block(self, blk, sems):
        for ename in ENGS:
            items = self.items[ename]
            sem_self = sems[ename]

            def play(eng, items=items, ename=ename, sem_self=sem_self):
                for it in items:
                    if it[0] == "w":
                        eng.wait_ge(sems[it[1]], it[2])
                    elif it[0] == "d":
                        eng.drain()
                    elif it[0] == "n":
                        eng.nop().then_inc(sem_self, 1)
                    else:
                        inst = it[1](eng)
                        if it[2]:
                            inst.then_inc(sem_self,
                                          16 if ename in ("qg", "qs") else 1)

            getattr(blk, BLOCKNAME[ename])(play)


def tsl(i, w=128):
    return slice(i * w, (i + 1) * w)



def build_program(stop_after="all", debug=False):
    nc = bass.Bass("TRN2", target_bir_lowering=False)

    def dram_in(name, shape, dtype=F32):
        return nc.dram_tensor(name, shape, dtype, kind="ExternalInput").ap()

    def dram_scratch(name, shape, dtype=F32):
        return nc.dram_tensor(name, shape, dtype, kind="Internal").ap()

    x0 = dram_in("x0", [S, D])
    wq_a = dram_in("wq_a", [H, D, DH]); wk_a = dram_in("wk_a", [H, D, DH])
    wv_a = dram_in("wv_a", [H, D, DH]); wo_a = dram_in("wo_a", [H, DH, D])
    bq_a = dram_in("bq_a", [128, H]); bk_a = dram_in("bk_a", [128, H])
    bv_a = dram_in("bv_a", [128, H]); bo_a = dram_in("bo_a", [1, D])
    wq_b = dram_in("wq_b", [H, D, DH]); wk_b = dram_in("wk_b", [H, D, DH])
    wv_b = dram_in("wv_b", [H, D, DH]); wo_b = dram_in("wo_b", [H, DH, D])
    betas = dram_in("betas", [128, H])
    wgT = dram_in("wgT", [D, DFF]); wuT = dram_in("wuT", [D, DFF])
    wdT = dram_in("wdT", [DFF, D])
    maskT = dram_in("maskT", [128, 128])
    ident = dram_in("ident", [128, 128])

    out = nc.dram_tensor("out", [S, D], F32, kind="ExternalOutput").ap()
    dbgf = dbgb = None
    dbg_names = {}
    if debug:
        dbgf = nc.dram_tensor("dbgf", [24, 128, 2048], F32, kind="ExternalOutput").ap()
        dbgb = nc.dram_tensor("dbgb", [24, 128, 2048], BF16, kind="ExternalOutput").ap()

    x1_scr = dram_scratch("x1_scr", [S, D])
    x2_scr = dram_scratch("x2_scr", [S, D])
    x3_scr = dram_scratch("x3_scr", [S, D])
    z_scr = dram_scratch("z_scr", [H, 128, S], BF16)
    att_scr = dram_scratch("att_scr", [H, 128, S], BF16)
    h_scr = dram_scratch("h_scr", [NDFF, 128, S])
    hq_scr = dram_scratch("hq_scr", [NDFF, 128, S], BF16)
    wgq = dram_scratch("wgq", [NDFF, NC, 128, 128], BF16)
    wuq = dram_scratch("wuq", [NDFF, NC, 128, 128], BF16)
    wdq = dram_scratch("wdq", [NDFF, 128, D], BF16)
    scr_sm = dram_scratch("scr_sm", [64, 32])
    scr_rm = dram_scratch("scr_rm", [128, 2048])

    xsb = {1: [Buf(f"x1s{t}") for t in range(NT)],
           2: [Buf(f"x2s{t}") for t in range(NT)],
           3: [Buf(f"x3s{t}") for t in range(NT)]}
    zsb = [Buf(f"zs{h}") for h in range(H)]
    attsb = [Buf(f"atts{h}") for h in range(H)]
    hsb = [Buf(f"hs{d}") for d in range(NDFF)]
    hqsb = [Buf(f"hqs{d}") for d in range(NDFF)]
    wgqb = [Buf(f"wgqb{d}") for d in range(NDFF)]
    wuqb = [Buf(f"wuqb{d}") for d in range(NDFF)]
    wdqb = [Buf(f"wdqb{d}") for d in range(NDFF)]
    rmb = Buf("scr_rm")
    xdst_b = {id(x1_scr): xsb[1], id(x2_scr): xsb[2], id(x3_scr): xsb[3],
              id(out): [Buf(f"outs{t}") for t in range(NT)],
              id(x0): [Buf(f"x0s{t}") for t in range(NT)]}

    P = Prog(nc)
    import contextlib
    stack = contextlib.ExitStack()

    def sbuf(name, shape, dtype):
        t = stack.enter_context(nc.sbuf_tensor(name, shape, dtype))
        return Tile(t, Buf(name))

    def psum(name, shape, dtype):
        t = stack.enter_context(nc.psum_tensor(name, shape, dtype))
        return Tile(t, Buf(name))

    BF = [sbuf(f"bf{i}", [128, 2048], BF16) for i in range(32)]
    FP = [sbuf(f"fp{i}", [128, 2048], F32) for i in range(5)]
    TI = sbuf("ti0", [128, 2048], I32)
    ROWA = sbuf("rowA", [128, 2048], F32)
    ROWB = sbuf("rowB", [128, 2048], F32)
    STA = sbuf("stA", [128, 64], F32)
    STB = sbuf("stB", [128, 64], F32)
    MEMS = sbuf("mems", [128, 2048], F32)
    MEMB = sbuf("memb", [128, 2048], BF16)
    ZST = sbuf("zst", [128, 64], F32)
    ZSTB = sbuf("zstb", [128, 64], BF16)
    MSK = sbuf("msk", [128, 128], F32)
    IDN = sbuf("idn", [128, 128], F32)
    ONEB = sbuf("oneb", [128, 128], BF16)
    ONEF = sbuf("onef", [128, 128], F32)
    BIAS = sbuf("biasa", [128, 64], F32)

    PS = [psum(f"ps{i}", [128, 512], F32) for i in range(8)]

    sems = {e: stack.enter_context(nc.semaphore(f"sem_{e}")) for e in ENGS}

    def dump(name, tile, ap=None, kind="b"):
        if not debug:
            return
        tgt = dbgb if kind == "b" else dbgf
        if name not in dbg_names:
            n = sum(1 for v in dbg_names.values() if v[0] == kind)
            dbg_names[name] = (kind, n)
        _, i = dbg_names[name]
        srcap = tile[:] if ap is None else ap
        shp = list(srcap.shape)
        dst = tgt[i]
        if shp[-1] != 2048 or shp[0] != 128:
            dst = tgt[i][0:shp[0], 0:shp[-1]]
        dma("qs", dst, srcap, reads=[tile])
    build_program.dbg_names = dbg_names


    def dma(queue, dst, src, reads=(), writes=()):
        P.emit(queue, lambda q, d=dst, s=src: q.dma_start(out=d, in_=s),
               reads=reads, writes=writes)

    def dmaT(dst, src, reads=(), writes=()):
        P.emit("qs", lambda q, d=dst, s=src: q.dma_start_transpose(d, s),
               reads=reads, writes=writes)

    def mm(outap, lhsT, rhs, reads, writes, start=True, stop=True, inc=False):
        P.emit("t", lambda t, o=outap, l=lhsT, r=rhs, st=start, sp=stop:
               t.matmul(o, l, r, start=st, stop=sp), reads=reads, writes=writes,
               inc=inc)

    def act(outap, inap, func, reads, writes, bias=None, scale=None, accum=None):
        def f(s, o=outap, i=inap, fu=func, b=bias, sc=scale, a=accum):
            kw = {}
            if b is not None:
                kw["bias"] = b
            if sc is not None:
                kw["scale"] = sc
            if a is not None:
                kw["accum_out"] = a
            return s.activation(o, i, fu, **kw)
        P.emit("s", f, reads=reads, writes=writes)

    def scopy(outap, inap, reads, writes):
        P.emit("s", lambda s, o=outap, i=inap: s.copy(o, i),
               reads=reads, writes=writes)

    def v_op(fn, reads, writes):
        P.emit("v", fn, reads=reads, writes=writes)

    dma("qg", MSK[:], maskT, writes=[MSK])
    dma("qg", IDN[:], ident, writes=[IDN])
    dma("qg", BIAS[:, 0:16], bq_a, writes=[BIAS])
    dma("qg", BIAS[:, 16:32], bk_a, writes=[BIAS])
    dma("qg", BIAS[:, 32:48], bv_a, writes=[BIAS])
    dma("qg", BIAS[:, 48:64], betas, writes=[BIAS])
    dma("qg", ROWA[96:97, :], bo_a, writes=[ROWA])
    P.emit("v", lambda v: v.memset(ONEB[:], 1.0), writes=[ONEB])
    P.emit("v", lambda v: v.memset(ONEF[:], 1.0), writes=[ONEF])
    v_op(lambda v: v.tensor_copy(MEMB[0:1, :], ROWA[96:97, :]),
         reads=[ROWA], writes=[MEMB])

    def ln_quant_tile(tq, x1f, quant, xnT_out, sact_base):
        xnf = FP[2]; tf = FP[3]; junk = FP[4]
        v_op(lambda v: v.reduce_sum(STA[:, tq:tq + 1], x1f[:], axis=AX),
             reads=[x1f], writes=[STA])
        act(junk[:], x1f[:], AF.Square, reads=[x1f], writes=[junk, STA],
            accum=STA[:, 16 + tq:17 + tq])
        c_mu = STA[:, tq:tq + 1]
        c_sq = STA[:, 16 + tq:17 + tq]
        c_var = STA[:, 48:49]; c_tmp = STA[:, 49:50]
        c_rs = STB[:, 0:1]; c_nm = STB[:, 1:2]
        v_op(lambda v: v.tensor_scalar_mul(c_mu, c_mu, 1.0 / D),
             reads=[STA], writes=[STA])
        v_op(lambda v: v.tensor_scalar_mul(c_sq, c_sq, 1.0 / D),
             reads=[STA], writes=[STA])
        v_op(lambda v: v.tensor_tensor(c_tmp, c_mu, c_mu, OP.mult),
             reads=[STA], writes=[STA])
        v_op(lambda v: v.tensor_tensor(c_var, c_sq, c_tmp, OP.subtract),
             reads=[STA], writes=[STA])
        v_op(lambda v: v.tensor_scalar_add(c_var, c_var, 1e-5),
             reads=[STA], writes=[STA])
        act(c_tmp, c_var, AF.Sqrt, reads=[STA], writes=[STA])
        P.emit("v", lambda v: v.reciprocal(c_rs, c_tmp), reads=[STA],
               writes=[STB])
        c_r2 = STB[:, 3:4]
        v_op(lambda v: v.tensor_tensor(c_r2, c_rs, c_rs, OP.mult),
             reads=[STB], writes=[STB])
        v_op(lambda v: v.tensor_tensor(c_r2, c_r2, c_var, OP.mult),
             reads=[STB, STA], writes=[STB])
        v_op(lambda v: v.tensor_scalar(c_r2, c_r2, -0.5, 1.5, OP.mult, OP.add),
             reads=[STB], writes=[STB])
        v_op(lambda v: v.tensor_tensor(c_rs, c_rs, c_r2, OP.mult),
             reads=[STB], writes=[STB])
        v_op(lambda v: v.tensor_tensor(c_nm, c_mu, c_rs, OP.mult),
             reads=[STA, STB], writes=[STB])
        v_op(lambda v: v.tensor_scalar_mul(c_nm, c_nm, -1.0),
             reads=[STB], writes=[STB])
        xnb = BF[30]
        if not quant:
            v_op(lambda v: v.tensor_scalar(xnb[:], x1f[:], c_rs, c_nm,
                                           OP.mult, OP.add),
                 reads=[x1f, STB], writes=[xnb])
        else:
            v_op(lambda v: v.tensor_scalar(xnf[:], x1f[:], c_rs, c_nm,
                                           OP.mult, OP.add),
                 reads=[x1f, STB], writes=[xnf])
            c_mx = STA[:, 50:51]
            c_inv = STB[:, 2:3]
            c_sact = STA[:, sact_base + tq:sact_base + tq + 1]
            v_op(lambda v: v.reduce_max(c_mx, xnf[:], axis=AX,
                                        apply_absolute_value=True),
                 reads=[xnf], writes=[STA])
            v_op(lambda v: v.tensor_scalar_max(c_mx, c_mx, 1e-5),
                 reads=[STA], writes=[STA])
            v_op(lambda v: v.tensor_scalar_mul(c_sact, c_mx, 1.0 / 127.0),
                 reads=[STA], writes=[STA])
            P.emit("v", lambda v: v.reciprocal(c_inv, c_sact),
                   reads=[STA], writes=[STB])
            v_op(lambda v: v.tensor_scalar_mul(tf[:], xnf[:], c_inv),
                 reads=[xnf, STB], writes=[tf])
            v_op(lambda v: v.tensor_scalar_add(TI[:], tf[:], 0.75),
                 reads=[tf], writes=[TI])
            v_op(lambda v: v.scalar_tensor_tensor(xnf[:], TI[:], -0.5, tf[:],
                                                  OP.add, OP.is_gt),
                 reads=[TI, tf], writes=[xnf])
            v_op(lambda v: v.tensor_tensor(xnb[:], TI[:], xnf[:], OP.subtract),
                 reads=[TI, xnf], writes=[xnb])
            pr = PS[7]
            mm(pr[0:1, 0:128], c_sact, IDN[:], reads=[STA, IDN], writes=[pr],
               inc=True)
            scopy(ROWB[32:33, tsl(tq)], pr[0:1, 0:128], reads=[pr], writes=[ROWB])
        for fc in range(NC):
            dmaT(xnT_out[fc][:, tsl(tq)], xnb[:, tsl(fc)],
                 reads=[xnb], writes=[xnT_out[fc]])

    def ln_stream(src_dram, quant, xnT_out, sact_base):
        xf = FP[0]
        sb = xdst_b.get(id(src_dram))
        for tq in range(NT):
            dma("qg", xf[:], src_dram[tsl(tq), :], writes=[xf],
                reads=[sb[tq]] if sb else [])
            ln_quant_tile(tq, xf, quant, xnT_out, sact_base)

    def out_proj(w_src, act_dram, act_bufs, x_src, x_dst, add_bias):
        wo_t = BF[0:16]
        for h in range(H):
            dma("qg", wo_t[h][:], w_src[h], writes=[wo_t[h]])
        zsl = [BF[25], BF[26]]
        xf = FP[0]; x1f = FP[1]
        for tq in range(NT):
            zt = zsl[tq % 2]
            dstap = zt[:].rearrange("p (h j) -> p h j", j=128)
            srcap = act_dram[:, :, tsl(tq)].rearrange("h p j -> p h j")
            dma("qg", dstap, srcap, writes=[zt], reads=list(act_bufs))
            dma("qg", xf[:], x_src[tsl(tq), :], writes=[xf],
                reads=[xdst_b[id(x_src)][tq]])
            oslots = [PS[(tq % 2) * 4 + j] for j in range(4)]
            for h in range(H):
                for j in range(4):
                    mm(oslots[j][:, 0:512], zt[:, tsl(h)], wo_t[h][:, tsl(j, 512)],
                       reads=[zt, wo_t[h]], writes=[oslots[j]],
                       start=(h == 0), stop=(not add_bias and h == H - 1),
                       inc=(not add_bias and h == H - 1 and j == 3))
            if add_bias:
                for j in range(4):
                    mm(oslots[j][:, 0:512], ONEB[0:1, :], MEMB[0:1, tsl(j, 512)],
                       reads=[ONEB, MEMB], writes=[oslots[j]],
                       start=False, stop=True, inc=(j == 3))
            for j in range(4):
                v_op(lambda v, j=j, osl=oslots[j]: v.tensor_tensor(
                    x1f[:, tsl(j, 512)], osl[:, 0:512], xf[:, tsl(j, 512)],
                    OP.add), reads=[oslots[j], xf], writes=[x1f])
            dma("qs", x_dst[tsl(tq), :], x1f[:], reads=[x1f],
                writes=[xdst_b[id(x_dst)][tq]])

    def attn_core_a(xnT, boundary_cb, x_src, x_dst):
        expt = BF[16:25]
        widths = [(16 - kc) * 128 for kc in range(16)]
        expmap = {}
        off = [0] * 9
        for kc in range(16):
            for i in range(9):
                if off[i] + widths[kc] <= 2048:
                    expmap[kc] = (expt[i], off[i])
                    off[i] += widths[kc]
                    break
            else:
                raise AssertionError("expT packing failed")
        qb = BF[25]; kb = BF[26]; vf = BF[27]; vt = BF[28]
        wqs = BF[29]; wks = BF[30]; wvs = BF[31]
        rrep = FP[4]

        def load_w(h):
            for (dst, src) in ((wqs, wq_a), (wks, wk_a), (wvs, wv_a)):
                ap = dst[:].rearrange("p (ic j) -> p ic j", j=128)
                srcap = src[h].rearrange("(ic p) j -> p ic j", p=128)
                dma("qg", ap, srcap, writes=[dst])

        load_w(0)
        for h in range(H):
            for (wbuf, dst, bias_col, pslots) in (
                (wqs, qb, BIAS[:, h:h + 1], PS[0:4]),
                (wks, kb, BIAS[:, 16 + h:17 + h], PS[4:8]),
                (wvs, vf, BIAS[:, 32 + h:33 + h], PS[0:4]),
            ):
                for ic in range(NC):
                    for j in range(4):
                        mm(pslots[j][:, 0:512], wbuf[:, tsl(ic)],
                           xnT[ic][:, tsl(j, 512)],
                           reads=[wbuf, xnT[ic]], writes=[pslots[j]],
                           start=(ic == 0), stop=(ic == NC - 1),
                           inc=(ic == NC - 1))
                for j in range(4):
                    v_op(lambda v, j=j, psl=pslots[j], d=dst, bc=bias_col:
                         v.tensor_scalar_add(d[:, tsl(j, 512)], psl[:, 0:512], bc),
                         reads=[pslots[j], BIAS], writes=[dst])
            for kc in range(NC):
                dmaT(vt[:, tsl(kc)], vf[:, tsl(kc)], reads=[vf], writes=[vt])
            if h == 0:
                dump("qb", qb); dump("kb", kb); dump("vf", vf); dump("vt", vt)
            if h + 1 < H:
                load_w(h + 1)
            q2 = vf
            for (src_t, row) in ((qb, 0), (kb, 32)):
                act(q2[:], src_t[:], AF.Square, reads=[src_t], writes=[q2])
                for j in range(4):
                    mm(PS[0][0:1, 0:512], ONEB[:, 0:1], q2[:, tsl(j, 512)],
                       reads=[ONEB, q2], writes=[PS[0]], inc=True)
                    scopy(ROWA[row:row + 1, tsl(j, 512)], PS[0][0:1, 0:512],
                          reads=[PS[0]], writes=[ROWA])
            cq = STA[0:1, 52:53]; ck = STA[0:1, 53:54]; cc = STA[0:1, 54:55]
            v_op(lambda v: v.reduce_max(cq, ROWA[0:1, :], axis=AX),
                 reads=[ROWA], writes=[STA])
            v_op(lambda v: v.reduce_max(ck, ROWA[32:33, :], axis=AX),
                 reads=[ROWA], writes=[STA])
            v_op(lambda v: v.tensor_tensor(cc, cq, ck, OP.add),
                 reads=[STA], writes=[STA])
            v_op(lambda v: v.tensor_scalar_mul(cc, cc, -0.5 * ISQ),
                 reads=[STA], writes=[STA])
            mm(PS[2][:, 0:1], ONEF[0:1, :], cc, reads=[ONEF, STA],
               writes=[PS[2]], inc=True)
            bcol = STA[:, 55:56]
            scopy(bcol, PS[2][:, 0:1], reads=[PS[2]], writes=[STA])
            ring = list(PS)
            ri = 0
            for kc in range(NC):
                et, eoff = expmap[kc]
                c0 = kc * 128
                c = c0
                while c < 2048:
                    ce = min(((c // 512) + 1) * 512, 2048)
                    w = ce - c
                    slot = ring[ri % 8]; ri += 1
                    mm(slot[:, 0:w], kb[:, tsl(kc)], qb[:, c:ce],
                       reads=[kb, qb], writes=[slot], inc=True)
                    if c == c0:
                        v_op(lambda v, sl=slot: v.tensor_tensor(
                            sl[:, 0:128], sl[:, 0:128], MSK[:], OP.add),
                            reads=[slot, MSK], writes=[slot])
                    act(et[:, eoff + (c - c0):eoff + (ce - c0)], slot[:, 0:w],
                        AF.Exp, reads=[slot, STA], writes=[et],
                        bias=bcol, scale=ISQ)
                    c = ce
            for j in range(4):
                kcs = [kc for kc in range(16) if kc * 128 < (j + 1) * 512]
                for idx, kc in enumerate(kcs):
                    et, eoff = expmap[kc]
                    c0 = kc * 128
                    a = max(c0, j * 512); bnd = (j + 1) * 512
                    mm(PS[0][0:1, (a - j * 512):(bnd - j * 512)], ONEB[:, 0:1],
                       et[:, eoff + (a - c0):eoff + (bnd - c0)],
                       reads=[ONEB, et], writes=[PS[0]],
                       start=(idx == 0), stop=(idx == len(kcs) - 1),
                       inc=(idx == len(kcs) - 1))
                scopy(ROWA[64:65, tsl(j, 512)], PS[0][0:1, 0:512],
                      reads=[PS[0]], writes=[ROWA])
            P.emit("v", lambda v: v.reciprocal(ROWB[0:1, :], ROWA[64:65, :]),
                   reads=[ROWA], writes=[ROWB])
            for j in range(4):
                mm(PS[1][:, 0:512], ONEF[0:1, :], ROWB[0:1, tsl(j, 512)],
                   reads=[ONEF, ROWB], writes=[PS[1]], inc=True)
                scopy(rrep[:, tsl(j, 512)], PS[1][:, 0:512],
                      reads=[PS[1]], writes=[rrep])
            zslots = [PS[4], PS[5], PS[6], PS[7]]
            for j in range(4):
                kcs = [kc for kc in range(16) if kc * 128 < (j + 1) * 512]
                for idx, kc in enumerate(kcs):
                    et, eoff = expmap[kc]
                    c0 = kc * 128
                    a = max(c0, j * 512); bnd = (j + 1) * 512
                    mm(zslots[j][:, (a - j * 512):(bnd - j * 512)], vt[:, tsl(kc)],
                       et[:, eoff + (a - c0):eoff + (bnd - c0)],
                       reads=[vt, et], writes=[zslots[j]],
                       start=(idx == 0), stop=(idx == len(kcs) - 1),
                       inc=(idx == len(kcs) - 1))
            zb = qb
            for j in range(4):
                v_op(lambda v, j=j: v.tensor_tensor(
                    zb[:, tsl(j, 512)], zslots[j][:, 0:512], rrep[:, tsl(j, 512)],
                    OP.mult), reads=[zslots[j], rrep], writes=[zb])
            if h == 0:
                dump("expt0", expt[0]); dump("expt1", expt[1])
                dump("rowa", ROWA, kind="f"); dump("rrep", rrep, kind="f")
                dump("sta", STA, STA[:, 0:64], kind="f")
                dump("zb", zb)
            dma("qs", z_scr[h], zb[:], reads=[zb], writes=[zsb[h]])

        out_proj(wo_a, z_scr, zsb, x_src, x_dst, add_bias=True)

    SG = STA[0:1, 60:61]; SU = STA[0:1, 61:62]; SD = STA[0:1, 62:63]
    THR = STA[:, 56:57]; NTHR = STA[:, 57:58]
    SDCOL = STA[:, 58:59]

    def ffn_prologue():
        acc = STA[:, 49:50]; tmp = STA[:, 50:51]
        for (m, wsrc, cell) in ((0, wgT, SG), (1, wuT, SU), (2, wdT, SD)):
            P.emit("v", lambda v: v.memset(acc, 0.0), writes=[STA])
            n_r, n_c = wsrc.shape
            tiles = [(r, c) for r in range(n_r // 128) for c in range(n_c // 2048)]
            for ti_, (r, c) in enumerate(tiles):
                xf = FP[ti_ % 2]
                dma("qg", xf[:], wsrc[tsl(r), tsl(c, 2048)], writes=[xf])
                v_op(lambda v, xf=xf: v.reduce_sum(tmp, xf[:], axis=AX,
                                                   apply_absolute_value=True),
                     reads=[xf], writes=[STA])
                v_op(lambda v: v.tensor_tensor(acc, acc, tmp, OP.add),
                     reads=[STA], writes=[STA])
            mm(PS[0][0:1, 0:1], acc, ONEF[:, 0:1], reads=[STA, ONEF],
               writes=[PS[0]], inc=True)
            scopy(cell, PS[0][0:1, 0:1], reads=[PS[0]], writes=[STA])
            v_op(lambda v, cell=cell: v.tensor_scalar(
                cell, cell, 1.0 / 16777216.0, 1e-5, OP.mult, OP.add),
                 reads=[STA], writes=[STA])
            thc = STA[0:1, 51:52]
            v_op(lambda v, cell=cell: v.tensor_scalar_mul(thc, cell, 0.5),
                 reads=[STA], writes=[STA])
            mm(PS[1][:, 0:1], ONEF[0:1, :], thc, reads=[ONEF, STA],
               writes=[PS[1]], inc=True)
            scopy(THR, PS[1][:, 0:1], reads=[PS[1]], writes=[STA])
            v_op(lambda v: v.tensor_scalar_mul(NTHR, THR, -1.0),
                 reads=[STA], writes=[STA])
            for ti_, (r, c) in enumerate(tiles):
                xf = FP[ti_ % 2]; p2 = FP[2 + ti_ % 2]; wq = BF[16 + ti_ % 2]
                dma("qg", xf[:], wsrc[tsl(r), tsl(c, 2048)], writes=[xf])
                v_op(lambda v, xf=xf, p2=p2: v.tensor_scalar(
                    p2[:], xf[:], NTHR, None, OP.is_le),
                     reads=[xf, STA], writes=[p2])
                v_op(lambda v, xf=xf, p2=p2, wq=wq: v.scalar_tensor_tensor(
                    wq[:], xf[:], THR, p2[:], OP.is_ge, OP.subtract),
                     reads=[xf, STA, p2], writes=[wq])
                if m < 2:
                    cache = wgq if m == 0 else wuq
                    cb = wgqb if m == 0 else wuqb
                    d0 = c * 16
                    dst = cache[d0:d0 + 16, r].rearrange("d p t -> p d t")
                    srcap = wq[:].rearrange("p (d t) -> p d t", t=128)
                    dma("qs", dst, srcap, reads=[wq],
                        writes=[cb[dd] for dd in range(d0, d0 + 16)])
                else:
                    di = r * (n_c // 2048) + c
                    dma("qs", wdq[di], wq[:], reads=[wq], writes=[wdqb[di]])

    def ffn_core(xqT, x_src, x_dst):
        sact = ROWB[32:33, :]
        A0 = ROWA[0:1, :]; B0 = ROWB[0:1, :]
        grep = FP[4]; gbuf = FP[2]; runmax = FP[3]
        v_op(lambda v: v.tensor_copy(A0, sact), reads=[ROWB], writes=[ROWA])
        v_op(lambda v: v.tensor_scalar_mul(B0, A0, SG),
             reads=[ROWA, STA], writes=[ROWB])
        for j in range(4):
            mm(PS[0][:, 0:512], ONEF[0:1, :], B0[0:1, tsl(j, 512)],
               reads=[ONEF, ROWB], writes=[PS[0]], inc=True)
            scopy(grep[:, tsl(j, 512)], PS[0][:, 0:512], reads=[PS[0]],
                  writes=[grep])
        P.emit("v", lambda v: v.memset(runmax[:], 0.0), writes=[runmax])
        wgt = [BF[16], BF[17]]; wut = [BF[18], BF[19]]
        hb = [FP[0], FP[1]]
        dump("xqT0", xqT[0]); dump("grep_g", grep, kind="f")
        for d in range(NDFF):
            wg_t = wgt[d % 2]; wu_t = wut[d % 2]; ht = hb[d % 2]
            dma("qg", wg_t[:].rearrange("p (ic t) -> p ic t", t=128),
                wgq[d].rearrange("ic p t -> p ic t"), writes=[wg_t],
                reads=[wgqb[d]])
            if d == 0:
                dump("wgt0", wg_t)
            dma("qg", wu_t[:].rearrange("p (ic t) -> p ic t", t=128),
                wuq[d].rearrange("ic p t -> p ic t"), writes=[wu_t],
                reads=[wuqb[d]])
            for ic in range(NC):
                for j in range(4):
                    mm(PS[j][:, 0:512], wg_t[:, tsl(ic)],
                       xqT[ic][:, tsl(j, 512)], reads=[wg_t, xqT[ic]],
                       writes=[PS[j]], start=(ic == 0), stop=(ic == NC - 1),
                       inc=(ic == NC - 1))
                    mm(PS[4 + j][:, 0:512], wu_t[:, tsl(ic)],
                       xqT[ic][:, tsl(j, 512)], reads=[wu_t, xqT[ic]],
                       writes=[PS[4 + j]], start=(ic == 0), stop=(ic == NC - 1),
                       inc=(ic == NC - 1))
            for j in range(4):
                v_op(lambda v, j=j: v.tensor_tensor(
                    gbuf[:, tsl(j, 512)], PS[j][:, 0:512], grep[:, tsl(j, 512)],
                    OP.mult), reads=[PS[j], grep], writes=[gbuf])
                act(gbuf[:, tsl(j, 512)], gbuf[:, tsl(j, 512)], AF.Sigmoid,
                    reads=[gbuf], writes=[gbuf])
                v_op(lambda v, j=j, ht=ht: v.tensor_tensor(
                    ht[:, tsl(j, 512)], gbuf[:, tsl(j, 512)], PS[4 + j][:, 0:512],
                    OP.mult), reads=[gbuf, PS[4 + j]], writes=[ht])
            if d == 0:
                dump("gbuf0", gbuf, kind="f"); dump("ht0", ht, kind="f")
            for j in range(4):
                act(PS[4 + j][:, 0:512], ht[:, tsl(j, 512)], AF.Abs,
                    reads=[ht], writes=[PS[4 + j]])
                v_op(lambda v, j=j: v.tensor_tensor(
                    runmax[:, tsl(j, 512)], runmax[:, tsl(j, 512)],
                    PS[4 + j][:, 0:512], OP.max),
                    reads=[runmax, PS[4 + j]], writes=[runmax])
            dma("qs", h_scr[d], ht[:], reads=[ht], writes=[hsb[d]])
        dma("qs", scr_rm, runmax[:], reads=[runmax], writes=[rmb])
        t1 = gbuf; t2 = hb[0]
        dma("qg", t1[0:32, 0:2048], scr_rm[0:32, :], writes=[t1], reads=[rmb])
        for g in range(1, 4):
            dma("qg", t2[0:32, 0:2048], scr_rm[tsl(g, 32), :], writes=[t2],
                reads=[rmb])
            v_op(lambda v: v.tensor_tensor(t1[0:32, :], t1[0:32, :],
                                           t2[0:32, :], OP.max),
                 reads=[t1, t2], writes=[t1])
        P.emit("v", lambda v: v.transpose(hb[1][0:32, :], t1[0:32, :]),
               reads=[t1], writes=[hb[1]])
        v_op(lambda v: v.reduce_max(
            STB[0:32, 0:64], hb[1][0:32, :].rearrange("p (j c) -> p j c", c=32),
            axis=AX), reads=[hb[1]], writes=[STB])
        dump("runmax", runmax, kind="f")
        for jc in range(4):
            for jj in range(16):
                j = 16 * jc + jj
                mm(PS[5][0:1, 32 * jj:32 * jj + 32], STB[0:32, j:j + 1],
                   IDN[0:32, 0:32], reads=[STB, IDN], writes=[PS[5]],
                   start=(jj == 0), stop=(jj == 15), inc=(jj == 15))
            scopy(B0[0:1, tsl(jc, 512)], PS[5][0:1, 0:512],
                  reads=[PS[5]], writes=[ROWB])
        dump("rowb", ROWB, kind="f")
        dump("sta", STA, STA[:, 0:64], kind="f")
        dump("stb32", STB, STB[:, 0:64], kind="f")
        v_op(lambda v: v.tensor_scalar_mul(A0, A0, SU),
             reads=[ROWA, STA], writes=[ROWA])
        v_op(lambda v: v.tensor_tensor(B0, B0, A0, OP.mult),
             reads=[ROWB, ROWA], writes=[ROWB])
        v_op(lambda v: v.tensor_scalar_max(B0, B0, 1e-5),
             reads=[ROWB], writes=[ROWB])
        mm(PS[2][:, 0:1], ONEF[0:1, :], SD, reads=[ONEF, STA], writes=[PS[2]],
           inc=True)
        scopy(SDCOL, PS[2][:, 0:1], reads=[PS[2]], writes=[STA])
        for tq in range(NT):
            mm(PS[3][:, 0:1], B0[0:1, tsl(tq)], ONEF[0:1, 0:1],
               reads=[ROWB, ONEF], writes=[PS[3]], inc=True)
            scopy(STA[:, tq:tq + 1], PS[3][:, 0:1], reads=[PS[3]], writes=[STA])
            v_op(lambda v, tq=tq: v.tensor_scalar(
                STB[:, 16 + tq:17 + tq], STA[:, tq:tq + 1], SDCOL, 1.0 / 127.0,
                OP.mult, OP.mult), reads=[STA], writes=[STB])
        v_op(lambda v: v.tensor_scalar_mul(A0, A0, 127.0),
             reads=[ROWA], writes=[ROWA])
        rcp_row = hb[1][0:1, 0:2048]
        P.emit("v", lambda v: v.reciprocal(rcp_row, B0), reads=[ROWB],
               writes=[hb[1]])
        v_op(lambda v: v.tensor_tensor(A0, A0, rcp_row, OP.mult),
             reads=[ROWA, hb[1]], writes=[ROWA])
        for j in range(4):
            mm(PS[1][:, 0:512], ONEF[0:1, :], A0[0:1, tsl(j, 512)],
               reads=[ONEF, ROWA], writes=[PS[1]], inc=True)
            scopy(grep[:, tsl(j, 512)], PS[1][:, 0:512], reads=[PS[1]],
                  writes=[grep])
        hq_t = BF[29]; tbuf = FP[2]; cbuf = FP[3]
        for d in range(NDFF):
            ht = hb[d % 2]
            dma("qg", ht[:], h_scr[d], writes=[ht], reads=[hsb[d]])
            v_op(lambda v, ht=ht: v.tensor_tensor(tbuf[:], ht[:], grep[:],
                                                  OP.mult),
                 reads=[ht, grep], writes=[tbuf])
            v_op(lambda v: v.tensor_scalar_add(TI[:], tbuf[:], 0.75),
                 reads=[tbuf], writes=[TI])
            v_op(lambda v: v.scalar_tensor_tensor(cbuf[:], TI[:], -0.5, tbuf[:],
                                                  OP.add, OP.is_gt),
                 reads=[TI, tbuf], writes=[cbuf])
            v_op(lambda v: v.tensor_tensor(hq_t[:], TI[:], cbuf[:], OP.subtract),
                 reads=[TI, cbuf], writes=[hq_t])
            if d == 0:
                dump("hq0", hq_t); dump("ratio_rep", grep, kind="f")
                dump("stb", STB, STB[:, 0:64], kind="f")
            dma("qs", hq_scr[d], hq_t[:], reads=[hq_t], writes=[hqsb[d]])
        wdt = BF[0:16]
        hqc = [BF[20:24], BF[24:28]]
        xsl = [FP[0], FP[1]]; osl = [FP[2], FP[3]]
        for och in range(4):
            for mgrp in range(16):
                dst = wdt[mgrp][:].rearrange("p (d j) -> p d j", j=512)
                srcap = wdq[4 * mgrp:4 * mgrp + 4, :, tsl(och, 512)]\
                    .rearrange("d p j -> p d j")
                dma("qg", dst, srcap, writes=[wdt[mgrp]],
                    reads=[wdqb[dd] for dd in range(4 * mgrp, 4 * mgrp + 4)])
            for tq in range(NT):
                ht4 = hqc[tq % 2]
                for mg in range(4):
                    dst = ht4[mg][:].rearrange("p (d j) -> p d j", j=128)
                    srcap = hq_scr[16 * mg:16 * mg + 16, :, tsl(tq)]\
                        .rearrange("d p j -> p d j")
                    dma("qg", dst, srcap, writes=[ht4[mg]],
                        reads=[hqsb[dd] for dd in range(16 * mg, 16 * mg + 16)])
                xs_t = xsl[tq % 2]
                dma("qg", xs_t[:, 0:512], x_src[tsl(tq), tsl(och, 512)],
                    writes=[xs_t], reads=[xdst_b[id(x_src)][tq]])
                ps = PS[tq % 8]
                for d in range(NDFF):
                    mg, dl = d // 16, d % 16
                    wt_m, wt_s = d // 4, d % 4
                    mm(ps[:, 0:512], ht4[mg][:, tsl(dl)],
                       wdt[wt_m][:, tsl(wt_s, 512)],
                       reads=[ht4[mg], wdt[wt_m]], writes=[ps],
                       start=(d == 0), stop=(d == NDFF - 1),
                       inc=(d == NDFF - 1))
                ot = osl[tq % 2]
                v_op(lambda v, ps=ps, ot=ot, tq=tq, xs_t=xs_t:
                     v.scalar_tensor_tensor(ot[:, 0:512], ps[:, 0:512],
                                            STB[:, 16 + tq:17 + tq],
                                            xs_t[:, 0:512], OP.mult, OP.add),
                     reads=[ps, STB, xs_t], writes=[ot])
                dma("qs", x_dst[tsl(tq), tsl(och, 512)], ot[:, 0:512],
                    reads=[ot], writes=[xdst_b[id(x_dst)][tq]])

    def attn_core_b(xnT, x_src, x_dst):
        qf = FP[0]; kf = FP[1]; t1 = FP[2]; t2 = FP[3]; reps = FP[4]
        qb = BF[25]; kb = BF[26]; vf = BF[27]; vt = BF[28]
        sq = BF[29]; sk = BF[30]; skt = BF[31]
        estg = BF[17]; ab = BF[16]; dl = BF[18]; q2 = BF[22]
        wqs = BF[19]; wks = BF[20]; wvs = BF[21]
        P.emit("v", lambda v: v.memset(MEMS[:], 0.0), writes=[MEMS])
        P.emit("v", lambda v: v.memset(MEMB[:], 0.0), writes=[MEMB])
        P.emit("v", lambda v: v.memset(ZST[:, 0:16], 1.0 / 128.0), writes=[ZST])
        v_op(lambda v: v.tensor_copy(ZSTB[:, 0:16], ZST[:, 0:16]),
             reads=[ZST], writes=[ZSTB])
        act(STB[:, 32:48], BIAS[:, 48:64], AF.Sigmoid, reads=[BIAS], writes=[STB])
        v_op(lambda v: v.tensor_scalar(STB[:, 48:64], STB[:, 32:48], -1.0, 1.0,
                                       OP.mult, OP.add),
             reads=[STB], writes=[STB])
        soff = [0, 512, 896, 1152]
        swid = [512, 384, 256, 128]

        def load_w(h):
            for (dst, srcw) in ((wqs, wq_b), (wks, wk_b), (wvs, wv_b)):
                ap = dst[:].rearrange("p (ic j) -> p ic j", j=128)
                srcap = srcw[h].rearrange("(ic p) j -> p ic j", p=128)
                dma("qg", ap, srcap, writes=[dst])

        load_w(0)
        for h in range(H):
            hb_sl = tsl(h)
            for (wbuf, dstf, dstb, pslots) in (
                (wqs, qf, qb, PS[0:4]),
                (wks, kf, kb, PS[4:8]),
                (wvs, None, vf, PS[0:4]),
            ):
                for ic in range(NC):
                    for j in range(4):
                        mm(pslots[j][:, 0:512], wbuf[:, tsl(ic)],
                           xnT[ic][:, tsl(j, 512)],
                           reads=[wbuf, xnT[ic]], writes=[pslots[j]],
                           start=(ic == 0), stop=(ic == NC - 1),
                           inc=(ic == NC - 1))
                for j in range(4):
                    if dstf is not None:
                        scopy(dstf[:, tsl(j, 512)], pslots[j][:, 0:512],
                              reads=[pslots[j]], writes=[dstf])
                        v_op(lambda v, j=j, dstf=dstf, dstb=dstb:
                             v.tensor_copy(dstb[:, tsl(j, 512)],
                                           dstf[:, tsl(j, 512)]),
                             reads=[dstf], writes=[dstb])
                    else:
                        v_op(lambda v, j=j, psl=pslots[j], dstb=dstb:
                             v.tensor_copy(dstb[:, tsl(j, 512)], psl[:, 0:512]),
                             reads=[pslots[j]], writes=[dstb])
            for kc in range(NC):
                dmaT(vt[:, tsl(kc)], vf[:, tsl(kc)], reads=[vf], writes=[vt])
            if h + 1 < H:
                load_w(h + 1)
            for (xf_, dstb) in ((qf, sq), (kf, sk)):
                act(t1[:], xf_[:], AF.Relu, reads=[xf_], writes=[t1])
                v_op(lambda v, xf_=xf_: v.tensor_tensor(t2[:], xf_[:], t1[:],
                                                        OP.subtract),
                     reads=[xf_, t1], writes=[t2])
                act(t2[:], t2[:], AF.Exp, reads=[t2], writes=[t2])
                v_op(lambda v, dstb=dstb: v.tensor_tensor(dstb[:], t2[:], t1[:],
                                                          OP.add),
                     reads=[t2, t1], writes=[dstb])
            for kc in range(NC):
                dmaT(skt[:, tsl(kc)], sk[:, tsl(kc)], reads=[sk], writes=[skt])
            if h == 0:
                dump("b_qb", qb); dump("b_sq", sq); dump("b_sk", sk)
                dump("b_vt", vt)
            for (src_t, row) in ((qb, 0), (kb, 32)):
                act(q2[:], src_t[:], AF.Square, reads=[src_t], writes=[q2])
                for j in range(4):
                    mm(PS[0][0:1, 0:512], ONEB[:, 0:1], q2[:, tsl(j, 512)],
                       reads=[ONEB, q2], writes=[PS[0]], inc=True)
                    scopy(ROWA[row:row + 1, tsl(j, 512)], PS[0][0:1, 0:512],
                          reads=[PS[0]], writes=[ROWA])
            cq = STA[0:1, 52:53]; ck = STA[0:1, 53:54]; cc = STA[0:1, 54:55]
            v_op(lambda v: v.reduce_max(cq, ROWA[0:1, :], axis=AX),
                 reads=[ROWA], writes=[STA])
            v_op(lambda v: v.reduce_max(ck, ROWA[32:33, :], axis=AX),
                 reads=[ROWA], writes=[STA])
            v_op(lambda v: v.tensor_tensor(cc, cq, ck, OP.add),
                 reads=[STA], writes=[STA])
            v_op(lambda v: v.tensor_scalar_mul(cc, cc, -0.5 * ISQ),
                 reads=[STA], writes=[STA])
            mm(PS[2][:, 0:1], ONEF[0:1, :], cc, reads=[ONEF, STA],
               writes=[PS[2]], inc=True)
            bcol = STA[:, 55:56]
            scopy(bcol, PS[2][:, 0:1], reads=[PS[2]], writes=[STA])
            betacol = STB[:, 32 + h:33 + h]
            ombcol = STB[:, 48 + h:49 + h]
            for sg in range(NSEG):
                sl = slice(sg * 512, sg * 512 + 512)
                blk0 = sg * 4
                for kc in range(4):
                    c0 = kc * 128
                    w = 512 - c0
                    slot = PS[kc]
                    mm(slot[:, 0:w], kb[:, tsl(blk0 + kc)],
                       qb[:, sg * 512 + c0:sg * 512 + 512],
                       reads=[kb, qb], writes=[slot], inc=True)
                    v_op(lambda v, sl_=slot: v.tensor_tensor(
                        sl_[:, 0:128], sl_[:, 0:128], MSK[:], OP.add),
                        reads=[slot, MSK], writes=[slot])
                    act(estg[:, soff[kc]:soff[kc] + w], slot[:, 0:w],
                        AF.Exp, reads=[slot, STA], writes=[estg],
                        bias=bcol, scale=ISQ)
                for kc in range(4):
                    mm(PS[4][0:1, kc * 128:512], ONEB[:, 0:1],
                       estg[:, soff[kc]:soff[kc] + swid[kc]],
                       reads=[ONEB, estg], writes=[PS[4]],
                       start=(kc == 0), stop=(kc == 3), inc=(kc == 3))
                scopy(ROWA[64:65, 0:512], PS[4][0:1, 0:512],
                      reads=[PS[4]], writes=[ROWA])
                P.emit("v", lambda v: v.reciprocal(ROWB[64:65, 0:512],
                                                   ROWA[64:65, 0:512]),
                       reads=[ROWA], writes=[ROWB])
                mm(PS[5][:, 0:512], ONEF[64:65, :], ROWB[64:65, 0:512],
                   reads=[ONEF, ROWB], writes=[PS[5]], inc=True)
                scopy(reps[:, 0:512], PS[5][:, 0:512], reads=[PS[5]],
                      writes=[reps])
                for kc in range(4):
                    mm(PS[4][:, kc * 128:512], vt[:, tsl(blk0 + kc)],
                       estg[:, soff[kc]:soff[kc] + swid[kc]],
                       reads=[vt, estg], writes=[PS[4]],
                       start=(kc == 0), stop=(kc == 3), inc=(kc == 3))
                mm(PS[5][:, 0:512], MEMB[:, hb_sl], sq[:, sl],
                   reads=[MEMB, sq], writes=[PS[5]], inc=True)
                mm(PS[6][0:1, 0:512], ZSTB[:, h:h + 1], sq[:, sl],
                   reads=[ZSTB, sq], writes=[PS[6]], inc=True)
                scopy(ROWA[64:65, 512:1024], PS[6][0:1, 0:512],
                      reads=[PS[6]], writes=[ROWA])
                P.emit("v", lambda v: v.reciprocal(ROWB[64:65, 512:1024],
                                                   ROWA[64:65, 512:1024]),
                       reads=[ROWA], writes=[ROWB])
                mm(PS[6][:, 0:512], ONEF[64:65, :], ROWB[64:65, 512:1024],
                   reads=[ONEF, ROWB], writes=[PS[6]], inc=True)
                scopy(reps[:, 512:1024], PS[6][:, 0:512], reads=[PS[6]],
                      writes=[reps])
                v_op(lambda v: v.tensor_tensor(t1[:, 0:512], PS[5][:, 0:512],
                                               reps[:, 512:1024], OP.mult),
                     reads=[PS[5], reps], writes=[t1])
                v_op(lambda v: v.tensor_tensor(t2[:, 0:512], PS[4][:, 0:512],
                                               reps[:, 0:512], OP.mult),
                     reads=[PS[4], reps], writes=[t2])
                v_op(lambda v, bc=betacol: v.tensor_scalar_mul(
                    t1[:, 0:512], t1[:, 0:512], bc),
                     reads=[t1, STB], writes=[t1])
                v_op(lambda v, sg=sg, ob=ombcol: v.scalar_tensor_tensor(
                    ab[:, tsl(sg, 512)], t2[:, 0:512], ob, t1[:, 0:512],
                    OP.mult, OP.add), reads=[t2, STB, t1], writes=[ab])
                for st in range(4):
                    mm(PS[6][:, st:st + 1], sk[:, tsl(blk0 + st)],
                       ZSTB[:, h:h + 1], reads=[sk, ZSTB], writes=[PS[6]],
                       inc=(st == 3))
                scopy(STA[:, 32:36], PS[6][:, 0:4], reads=[PS[6]], writes=[STA])
                P.emit("v", lambda v: v.reciprocal(STB[:, 4:8], STA[:, 32:36]),
                       reads=[STA], writes=[STB])
                for st in range(4):
                    mm(PS[7][:, tsl(st)], sk[:, tsl(blk0 + st)],
                       MEMB[:, hb_sl], reads=[sk, MEMB], writes=[PS[7]],
                       inc=(st == 3))
                for st in range(4):
                    v_op(lambda v, st=st: v.tensor_scalar_mul(
                        t1[:, tsl(st)], PS[7][:, tsl(st)], STB[:, 4 + st:5 + st]),
                         reads=[PS[7], STB], writes=[t1])
                    v_op(lambda v, st=st, blk=blk0 + st: v.tensor_tensor(
                        dl[:, tsl(st)], vt[:, tsl(blk)], t1[:, tsl(st)],
                        OP.subtract), reads=[vt, t1], writes=[dl])
                for st in range(4):
                    mm(PS[5][:, 0:128], skt[:, tsl(blk0 + st)], dl[:, tsl(st)],
                       reads=[skt, dl], writes=[PS[5]],
                       start=(st == 0), stop=(st == 3), inc=(st == 3))
                v_op(lambda v, hs=hb_sl: v.tensor_tensor(
                    MEMS[:, hs], MEMS[:, hs], PS[5][:, 0:128], OP.add),
                     reads=[MEMS, PS[5]], writes=[MEMS])
                v_op(lambda v, hs=hb_sl: v.tensor_copy(MEMB[:, hs], MEMS[:, hs]),
                     reads=[MEMS], writes=[MEMB])
                v_op(lambda v, sl=sl: v.reduce_sum(STB[:, 8:9], sk[:, sl],
                                                   axis=AX),
                     reads=[sk], writes=[STB])
                v_op(lambda v, h=h: v.tensor_tensor(
                    ZST[:, h:h + 1], ZST[:, h:h + 1], STB[:, 8:9], OP.add),
                     reads=[ZST, STB], writes=[ZST])
                v_op(lambda v, h=h: v.tensor_copy(ZSTB[:, h:h + 1],
                                                  ZST[:, h:h + 1]),
                     reads=[ZST], writes=[ZSTB])
                if h == 0:
                    dump(f"b_mem{sg}", MEMS, MEMS[:, 0:128], kind="f")
                    dump(f"b_dl{sg}", dl, dl[:, 0:512])
            if h in (0, 1, 15):
                dump(f"b_ab{h}" if h else "b_ab", ab)
            if h == 0:
                dump("b_estg", estg)
                dump("b_zst", ZST, ZST[:, 0:16], kind="f")
                dump("b_reps", reps, kind="f")
            abR = BF[23]
            for sg in range(NSEG):
                v_op(lambda v, sg=sg: v.tensor_copy(
                    abR[:, tsl(sg, 512)].rearrange("p (hb sb) -> p hb sb", sb=32),
                    ab[:, tsl(sg, 512)].rearrange("p (sb hb) -> p hb sb", hb=16)),
                    reads=[ab], writes=[abR])
            dstap = att_scr.rearrange("hb p (sg t) -> hb p sg t", sg=NSEG)[
                :, :, :, h * 32:h * 32 + 32].rearrange("hb p sg sb -> p sg hb sb")
            srcap = abR[:].rearrange("p (sg hb sb) -> p sg hb sb", sg=NSEG, sb=32)
            P.emit("qs", lambda q, d=dstap, s=srcap: q.dma_start(out=d, in_=s),
                   reads=[abR], writes=[attsb[h]])
        out_proj(wo_b, att_scr, attsb, x_src, x_dst, add_bias=False)

    xnT_a = BF[0:16]
    if stop_after == "attn_a":
        ln_stream(x0, False, xnT_a, 0)
        dump("xnT0", xnT_a[0]); dump("xnT15", xnT_a[15])
        attn_core_a(xnT_a, None, x0, out)
    elif stop_after == "ffn1":
        ffn_prologue()
        ln_stream(x0, False, xnT_a, 0)
        attn_core_a(xnT_a, None, x0, x1_scr)
        ln_stream(x1_scr, True, BF[0:16], 32)
        ffn_core(BF[0:16], x1_scr, out)
    else:
        ffn_prologue()
        ln_stream(x0, False, xnT_a, 0)
        attn_core_a(xnT_a, None, x0, x1_scr)
        ln_stream(x1_scr, True, BF[0:16], 32)
        ffn_core(BF[0:16], x1_scr, x2_scr)
        ln_stream(x2_scr, False, BF[0:16], 0)
        attn_core_b(BF[0:16], x2_scr, out if stop_after == "attn_b" else x3_scr)
        if stop_after != "attn_b":
            ln_stream(x3_scr, True, BF[0:16], 32)
            ffn_core(BF[0:16], x3_scr, out)

    P.items["qs"].append(("w", "qs", P.inc_count["qs"]))

    with nc.Block() as blk:
        P.finalize_into_block(blk, sems)
    stack.close()
    return nc



def _prep_inputs(inputs):
    f = np.ascontiguousarray
    base = {
        "wq_a": f(inputs["L0_attn_a_W_Q"]),
        "wk_a": f(inputs["L0_attn_a_W_K"]),
        "wv_a": f(inputs["L0_attn_a_W_V"]),
        "wo_a": f(inputs["L0_attn_a_W_O"]),
        "bq_a": f(np.asarray(inputs["L0_attn_a_b_Q"]).T),
        "bk_a": f(np.asarray(inputs["L0_attn_a_b_K"]).T),
        "bv_a": f(np.asarray(inputs["L0_attn_a_b_V"]).T),
        "bo_a": f(np.asarray(inputs["L0_attn_a_b_O"])[None, :]),
        "wq_b": f(np.asarray(inputs["L0_attn_b_Wq"]).reshape(D, H, DH).transpose(1, 0, 2)),
        "wk_b": f(np.asarray(inputs["L0_attn_b_Wk"]).reshape(D, H, DH).transpose(1, 0, 2)),
        "wv_b": f(np.asarray(inputs["L0_attn_b_Wv"]).reshape(D, H, DH).transpose(1, 0, 2)),
        "wo_b": f(np.asarray(inputs["L0_attn_b_Wo"]).reshape(H, DH, D)),
        "betas": f(np.asarray(inputs["L0_attn_b_betas"]).reshape(H, DH).T),
        "wgT": f(np.asarray(inputs["L0_ffn_w_gate"]).T),
        "wuT": f(np.asarray(inputs["L0_ffn_w_up"]).T),
        "wdT": f(np.asarray(inputs["L0_ffn_w_down"]).T),
        "maskT": np.ascontiguousarray(
            np.where(np.tril(np.ones((128, 128), bool)), 0.0, NEGBIG).T
        ).astype(np.float32),
        "ident": np.eye(128, dtype=np.float32),
    }
    base = {k: np.asarray(v, dtype=base[k].dtype if hasattr(base[k], 'dtype') else np.float32)
            for k, v in base.items()}
    x = np.asarray(inputs["L0_attn_a_x"], dtype=np.float32)
    maps = []
    for c in range(8):
        m = dict(base)
        m["x0"] = f(x[c])
        maps.append(m)
    return maps


def kernel(_stop_after="all", _debug=False, **inputs):
    nc = build_program(stop_after=_stop_after, debug=_debug)
    in_maps = _prep_inputs(inputs)
    res = run_bass_kernel_spmd(nc, in_maps, core_ids=list(range(8)))
    outs = np.stack([np.asarray(r["out"], dtype=np.float32)
                     for r in res.results], axis=0)
    if _debug:
        kernel.dbg = (build_program.dbg_names,
                      {k: res.results[0][k] for k in ("dbgf", "dbgb")})
    return outs
